# version 06 (final)
# speedup vs baseline: 1.3171x; 1.0223x over previous
import numpy as np
import concourse.bass as bass
import concourse.mybir as mybir
from concourse.bass_utils import run_bass_kernel_spmd

F32 = mybir.dt.float32
BF16 = mybir.dt.bfloat16
AL = mybir.AluOpType
AF = mybir.ActivationFunctionType

NCORES = 8
T_FULL = 16384
C = 1024
HID = 4096
P = 128
NCH = C // P
TLOC = T_FULL // NCORES
SEG = 512
NSEG = TLOC // SEG
NTB = SEG // P
HSW = SEG + 2
SW = SEG + 1
LN_EPS = 1e-5
K_CLAMP = 60.0
NSLAB = HID // SEG

DEBUG = ()


class Sem:
    def __init__(self, nc, name):
        self.h = nc.alloc_semaphore(name=name)
        self.n = 0

    def bump(self, amt):
        self.n += amt
        return self.n

    def total(self):
        return (self, self.n)


class Prog:
    ENGINES = ("sync", "tensor", "vector", "scalar", "gpsimd")

    def __init__(self, nc):
        self.nc = nc
        self.items = {e: [] for e in self.ENGINES}
        self.hw = {e: {} for e in self.ENGINES}

    def emit(self, eng, fn, waits=(), inc=None, inc_amt=1):
        real = []
        for wt in waits:
            if wt is None:
                continue
            s, v = wt
            if v is None or v <= self.hw[eng].get(s, 0):
                continue
            real.append((s, v))
            self.hw[eng][s] = v
        mile = None
        if inc is not None:
            mile = (inc, inc.bump(inc_amt))
        self.items[eng].append((real, fn, inc, inc_amt))
        return mile

    def wait(self, eng, *miles):
        ws = [m for m in miles if m is not None]
        if ws:
            self.emit(eng, None, waits=ws)

    def install(self, block):
        for eng_name in self.ENGINES:
            items = self.items[eng_name]

            def make(items):
                def body(eng):
                    for waits, fn, inc, amt in items:
                        for (s, v) in waits:
                            eng.wait_ge(s.h, v)
                        instr = fn(eng) if fn is not None else None
                        if inc is not None:
                            if instr is None:
                                instr = eng.engine_nop()
                            instr.then_inc(inc.h, amt)
                return body

            getattr(block, eng_name)(make(items))


class Arena:
    def __init__(self, nc, words, name="arena"):
        t = nc.alloc_sbuf_tensor(name, [P, words], F32)
        self.ap = t.ap()
        self.ap16 = t.bitcast(BF16).ap()
        self.words = words
        self.off = 0
        self.marks = []
        self.peak = 0

    def alloc(self, w, align=8):
        self.off = (self.off + align - 1) // align * align
        assert self.off + w <= self.words, (
            f"SBUF arena overflow: {self.off}+{w} > {self.words}")
        a = self.ap[:, self.off:self.off + w]
        self.off += w
        self.peak = max(self.peak, self.off)
        return a

    def alloc16(self, n16, align=8):
        w = (n16 + 1) // 2
        self.off = (self.off + align - 1) // align * align
        assert self.off + w <= self.words, (
            f"SBUF arena overflow: {self.off}+{w} > {self.words}")
        a = self.ap16[:, 2 * self.off:2 * self.off + n16]
        self.off += w
        self.peak = max(self.peak, self.off)
        return a

    def push(self):
        self.marks.append(self.off)

    def pop(self):
        self.off = self.marks.pop()


def build_nc(debug=DEBUG):
    nc = bass.Bass("TRN2", target_bir_lowering=False)

    def din(name, shape):
        return nc.dram_tensor(name, shape, F32, kind="ExternalInput").ap()

    xs = din("xs", [TLOC + 2, C])
    wkT = din("wkT", [C, C]);  wvT = din("wvT", [C, C])
    wrT = din("wrT", [C, C]);  woT = din("woT", [C, C])
    fkT = din("fkT", [C, HID]); fvT = din("fvT", [HID, C])
    frT = din("frT", [C, C])
    w3rT = din("w3rT", [C, C]); w3kT = din("w3kT", [C, C])
    w3vT = din("w3vT", [C, C])
    pv = din("pv", [P, 12 * NCH])
    masks = din("masks", [P, 2 * NCORES + 2])
    ones_d = din("ones_d", [P, P])
    ident_d = din("ident_d", [P, P])

    out_d = nc.dram_tensor("out", [TLOC, C], F32, kind="ExternalOutput").ap()

    dbg = {}
    for name in debug:
        shape = {"carry": [P, 2 * NCH * NSEG], "cx": [P, 256],
                 "slv": [P, 7 * NCH]}.get(name, [P, NCH * TLOC])
        if name == "w3d":
            dbg["w3d"] = nc.dram_tensor("dbg_w3d", [P, 48 * SEG], F32,
                                        kind="ExternalOutput").ap()
            continue
        if name == "xr":
            for extra_name in ("xk", "ka", "yf", "sg"):
                dbg[extra_name] = nc.dram_tensor(
                    "dbg_" + extra_name, [P, NCH * TLOC], F32,
                    kind="ExternalOutput").ap()
        dbg[name] = nc.dram_tensor("dbg_" + name, shape, F32,
                                   kind="ExternalOutput").ap()

    def scr(name, cols):
        return nc.dram_tensor(name, [P, cols], F32).ap()

    sp_xT = scr("sp_xT", NCH * TLOC)
    sp_h = scr("sp_h", NCH * TLOC)
    sp_num = scr("sp_num", NCH * TLOC)
    sp_den = scr("sp_den", NCH * TLOC)
    sp_w = scr("sp_w", NCH * TLOC)
    sp_x1 = scr("sp_x1", NCH * TLOC)
    sp_fkh = nc.dram_tensor("sp_fkh", [C, HID], BF16).ap()
    sp_fkl = nc.dram_tensor("sp_fkl", [C, HID], BF16).ap()
    sp_fvh = nc.dram_tensor("sp_fvh", [HID, C], BF16).ap()
    sp_fvl = nc.dram_tensor("sp_fvl", [HID, C], BF16).ap()
    sp_frh = nc.dram_tensor("sp_frh", [C, C], BF16).ap()
    sp_frl = nc.dram_tensor("sp_frl", [C, C], BF16).ap()
    ag1_in = scr("ag1_in", 2 * NCH)
    ag1_out = nc.dram_tensor("ag1_out", [NCORES * P, 2 * NCH], F32).ap()
    ag2a_in = scr("ag2a_in", NCH)
    ag2a_out = nc.dram_tensor("ag2a_out", [NCORES * P, NCH], F32).ap()
    ag2b_in = scr("ag2b_in", NCH)
    ag2b_out = nc.dram_tensor("ag2b_out", [NCORES * P, NCH], F32).ap()
    ag3_in = scr("ag3_in", NCH)
    ag3_out = nc.dram_tensor("ag3_out", [NCORES * P, NCH], F32).ap()

    AR = Arena(nc, nc.sbuf_bytes_remaining // 4 - 8)

    ones_t = AR.alloc(P)
    ident_t = AR.alloc(P)
    pvt = AR.alloc(12 * NCH)
    maskt = AR.alloc(2 * NCORES + 2)
    ED = AR.alloc(NCH); EU = AR.alloc(NCH)
    EDT512 = AR.alloc(NCH); EDT2048 = AR.alloc(NCH)
    OMTM = AR.alloc(NCH); OMTMC = AR.alloc(NCH)
    OMTMF = AR.alloc(NCH); OMTMFC = AR.alloc(NCH)
    OMUR = AR.alloc(NCH); OMUK = AR.alloc(NCH)
    hL = AR.alloc(NCH); hR = AR.alloc(NCH)
    Aa = AR.alloc(NCH * NSEG); Ab = AR.alloc(NCH * NSEG)
    Ia_a = AR.alloc(NCH * NSEG); Ia_b = AR.alloc(NCH * NSEG)
    carry_a = AR.alloc(NCH * NSEG); carry_b = AR.alloc(NCH * NSEG)
    powg = AR.alloc(NCH * NSEG)
    uF = AR.alloc(NCH); uL = AR.alloc(NCH)
    wLs = AR.alloc(NCH); wRs = AR.alloc(NCH)
    x1L = AR.alloc(NCH)
    agb = AR.alloc(NCORES * 2 * NCH)
    scA = AR.alloc(2 * NCH); scB = AR.alloc(2 * NCH)
    scC = AR.alloc(2 * NCH); scD = AR.alloc(2 * NCH)

    def psum(name, w=SEG):
        return nc.alloc_psum_tensor(name, [P, w], F32).ap()

    pmm = [psum("pmm0"), psum("pmm1")]
    ptr = [psum("ptr0"), psum("ptr1")]
    pst = [psum("pst0"), psum("pst1")]
    pex = [psum("pex0"), psum("pex1")]

    PR = Prog(nc)
    sems = {}

    def sem(name):
        if name not in sems:
            sems[name] = Sem(nc, name)
        return sems[name]

    s_te = sem("s_te"); s_ve = sem("s_ve"); s_sc = sem("s_sc")
    s_cc = sem("s_cc")
    done = {}

    def pvcol(i):
        return pvt[:, i * NCH:(i + 1) * NCH]

    g1, b1, g2, b2 = pvcol(0), pvcol(1), pvcol(2), pvcol(3)
    atm, acm, td, tf = pvcol(4), pvcol(5), pvcol(6), pvcol(7)
    ftm, fcm, mur, muk = pvcol(8), pvcol(9), pvcol(10), pvcol(11)
    selprev = maskt[:, 0:NCORES]
    selnext = maskt[:, NCORES:2 * NCORES]
    maskl = maskt[:, 2 * NCORES:2 * NCORES + 1]
    maskr = maskt[:, 2 * NCORES + 1:2 * NCORES + 2]

    def ve(fn, waits=(), mile=False):
        return PR.emit("vector", fn, waits=waits, inc=s_ve if mile else None)

    def sc(fn, waits=(), mile=False):
        return PR.emit("scalar", fn, waits=waits, inc=s_sc if mile else None)

    def vef(fn, waits=()):
        m = PR.emit("vector", fn, waits=waits, inc=s_ve)
        PR.wait("vector", m)
        return m

    def scf(fn, waits=()):
        m = PR.emit("scalar", fn, waits=waits, inc=s_sc)
        PR.wait("scalar", m)
        return m

    def te(fn, waits=(), mile=False):
        return PR.emit("tensor", fn, waits=waits, inc=s_te if mile else None)

    def ld(stream, fn, waits=()):
        return PR.emit("sync", fn, waits=waits, inc=sem(stream), inc_amt=16)

    def gpd(stream, fn, waits=()):
        return PR.emit("gpsimd", fn, waits=waits, inc=sem(stream), inc_amt=16)

    def gpc(fn, waits=()):
        return PR.emit("gpsimd", fn, waits=waits, inc=s_cc, inc_amt=1)

    def emit_ln_stats(xv, width, st, wait_miles, double=False):
        narrow = width < 256

        def lve(fn, waits=(), mile=False):
            return vef(fn, waits) if narrow else ve(fn, waits, mile)

        def lsc(fn, waits=(), mile=False):
            return scf(fn, waits) if narrow else sc(fn, waits, mile)
        sq = [st["sq0"], st["sq1"]]
        m_prev = [None, None]
        m_px = None
        for dc in range(NCH):
            b = 0 if st.get("_1sq") else dc % 2
            m_sq = lsc(lambda e, dc=dc, b=b: e.square(sq[b][:, :width], xv(dc)),
                      waits=([m_prev[b]] +
                             (list(wait_miles) if dc == 0 else [])),
                      mile=True)
            if dc == 0:
                lve(lambda e, dc=dc: e.tensor_copy(st["Px"][:, :width], xv(dc)),
                   waits=list(wait_miles))
                m_add = lve(lambda e, b=b: e.tensor_copy(
                    st["Pxx"][:, :width], sq[b][:, :width]),
                    waits=[m_sq], mile=True)
            else:
                lve(lambda e, dc=dc: e.tensor_add(
                    st["Px"][:, :width], st["Px"][:, :width], xv(dc)))
                m_add = lve(lambda e, b=b: e.tensor_add(
                    st["Pxx"][:, :width], st["Pxx"][:, :width],
                    sq[b][:, :width]), waits=[m_sq], mile=True)
            if st.get("_1sq"):
                m_prev[0] = m_prev[1] = m_add
            else:
                m_prev[b] = m_add
            m_px = m_add
        m_bc0 = te(lambda e: e.matmul(pst[0][:, :width], ones_t,
                                      st["Px"][:, :width],
                                      start=True, stop=True),
                   waits=[m_px, done.get("pst0_free")], mile=True)
        m_bc1 = te(lambda e: e.matmul(pst[1][:, :width], ones_t,
                                      st["Pxx"][:, :width],
                                      start=True, stop=True),
                   waits=[done.get("pst1_free")], mile=True)
        inv_d = 1.0 / C
        m0 = lve(lambda e: e.tensor_scalar(st["mb"][:, :width],
                                          pst[0][:, :width], inv_d, None,
                                          AL.mult), waits=[m_bc0], mile=True)
        done["pst0_free"] = m0
        a_ = (1.0 + LN_EPS) * inv_d if double else inv_d
        b_ = LN_EPS * LN_EPS if double else LN_EPS
        m1 = lve(lambda e: e.tensor_scalar(st["t1"][:, :width],
                                          pst[1][:, :width], a_, b_,
                                          AL.mult, AL.add),
                waits=[m_bc1], mile=True)
        done["pst1_free"] = m1
        am = (1.0 + LN_EPS) if double else 1.0
        lve(lambda e: e.tensor_mul(st["s0"][:, :width], st["mb"][:, :width],
                                  st["mb"][:, :width]))
        m_t1 = lve(lambda e: e.scalar_tensor_tensor(
            st["t1"][:, :width], st["s0"][:, :width], -am,
            st["t1"][:, :width], AL.mult, AL.add), mile=True)
        lsc(lambda e: e.activation(st["s0"][:, :width], st["t1"][:, :width],
                                  AF.Ln), waits=[m_t1])
        m_e = lsc(lambda e: e.activation(st["sB"][:, :width],
                                        st["s0"][:, :width], AF.Exp,
                                        scale=-0.5), mile=True)
        lve(lambda e: e.tensor_mul(st["s0"][:, :width], st["sB"][:, :width],
                                  st["sB"][:, :width]), waits=[m_e])
        lve(lambda e: e.tensor_mul(st["s0"][:, :width], st["s0"][:, :width],
                                  st["t1"][:, :width]))
        lve(lambda e: e.tensor_scalar(st["s0"][:, :width], st["s0"][:, :width],
                                     -0.5, 1.5, AL.mult, AL.add))
        lve(lambda e: e.tensor_mul(st["sB"][:, :width], st["sB"][:, :width],
                                  st["s0"][:, :width]))
        m_ms = lve(lambda e: e.tensor_mul(st["msB"][:, :width],
                                         st["mb"][:, :width],
                                         st["sB"][:, :width]), mile=True)
        return m_ms

    def alloc_stats(width=SEG, single_sq=False):
        st = {k: AR.alloc(width) for k in
              ("sq0", "Px", "Pxx", "s0", "sB", "msB")}
        st["sq1"] = AR.alloc(width) if not single_sq else st["sq0"]
        st["_1sq"] = single_sq
        st["mb"] = st["Px"]
        st["t1"] = st["Pxx"]
        return st

    ld("s_c", lambda e: e.dma_start(out=pvt, in_=pv))
    ld("s_c", lambda e: e.dma_start(out=maskt, in_=masks))
    ld("s_c", lambda e: e.dma_start(out=ones_t, in_=ones_d))
    m_c0 = ld("s_c", lambda e: e.dma_start(out=ident_t, in_=ident_d))

    scf(lambda e: e.activation(scA[:, :NCH], td, AF.Exp), waits=[m_c0])
    m_ed = scf(lambda e: e.activation(ED, scA[:, :NCH], AF.Exp, scale=-1.0))
    m_eu = scf(lambda e: e.activation(EU, tf, AF.Exp))
    PR.wait("vector", m_ed, m_eu, m_c0)
    vef(lambda e: e.tensor_mul(EDT512, ED, ED))
    for _ in range(8):
        vef(lambda e: e.tensor_mul(EDT512, EDT512, EDT512))
    vef(lambda e: e.tensor_copy(EDT2048, EDT512))
    vef(lambda e: e.tensor_mul(EDT2048, EDT2048, EDT2048))
    vef(lambda e: e.tensor_mul(EDT2048, EDT2048, EDT2048))
    for dc in range(NCH):
        pg = powg[:, dc * NSEG:(dc + 1) * NSEG]
        vef(lambda e, pg=pg: e.memset(pg[:, 0:1], 1.0))
        for g in range(1, NSEG):
            vef(lambda e, pg=pg, g=g, dc=dc: e.tensor_scalar(
                pg[:, g:g + 1], pg[:, g - 1:g], EDT512[:, dc:dc + 1], None,
                AL.mult))
    vef(lambda e: e.tensor_scalar(OMTM, atm, -1.0, 1.0, AL.mult, AL.add))
    vef(lambda e: e.tensor_add(OMTMC, OMTM, acm))
    vef(lambda e: e.tensor_scalar(OMTMF, ftm, -1.0, 1.0, AL.mult, AL.add))
    vef(lambda e: e.tensor_add(OMTMFC, OMTMF, fcm))
    vef(lambda e: e.tensor_scalar(OMUR, mur, -1.0, 1.0, AL.mult, AL.add))
    m_p0 = vef(lambda e: e.tensor_scalar(OMUK, muk, -1.0, 1.0,
                                         AL.mult, AL.add))
    PR.wait("scalar", m_p0)

    AR.push()
    wk_t = AR.alloc(NCH * C)
    wv_s = [AR.alloc(4 * C) for _ in range(2)]
    xload = [AR.alloc(C) for _ in range(2)]
    xTs = AR.alloc(NCH * SEG)
    st1 = alloc_stats(single_sq=True)
    hs = [AR.alloc(NCH * HSW) for _ in range(2)]
    xm = AR.alloc(NCH * SEG)
    kt2 = [AR.alloc(NCH * SEG) for _ in range(2)]
    sb_t = AR.alloc(NCH * SW)
    numt = [AR.alloc(SEG) for _ in range(2)]
    dent = [AR.alloc(SEG) for _ in range(2)]

    m_wk = ld("s_wkv", lambda e: e.dma_start(
        out=wk_t.rearrange("p (dc c) -> p dc c", dc=NCH),
        in_=wkT.rearrange("(dc p) c -> p dc c", p=P)))
    def load_wv_half(g, half):
        slot = half
        return ld(f"s_wv{slot}", lambda e, half=half, slot=slot: e.dma_start(
            out=wv_s[slot].rearrange("p (dc c) -> p dc c", dc=4),
            in_=wvT.rearrange("(dc p) c -> p dc c", p=P)[
                :, half * 4:half * 4 + 4, :]),
            waits=[done.get(("wv_free", slot))])

    def wv_tile(dc, cc):
        return wv_s[dc // 4][:, (dc % 4) * C + cc * P:
                             (dc % 4) * C + (cc + 1) * P]

    def wtile(wt, dc, cc):
        return wt[:, dc * C + cc * P:dc * C + (cc + 1) * P]

    ve(lambda e: e.memset(
        sb_t.rearrange("p (dc w) -> p dc w", w=SW)[:, :, 0:1], 0.0))

    trp = [0]

    def emit_block_transposes(src_view_fn, nblk, waits, per_block_waits=None):
        m_evs = []
        m_tb = {}
        for dc in range(NCH):
            bank = trp[0] % 2
            trp[0] += 1
            wts = ([done.get(("ptr_free", bank))] +
                   (list(waits) if dc == 0 else []))
            m_t = None
            for tb in range(nblk):
                if per_block_waits is not None:
                    wts = wts + [per_block_waits[tb]]
                m_t = te(lambda e, dc=dc, tb=tb, bank=bank:
                         e.transpose(ptr[bank][:, tb * P:(tb + 1) * P],
                                     src_view_fn(tb, dc), ident_t),
                         waits=wts, mile=(tb == nblk - 1))
                wts = []
            m_ev = sc(lambda e, dc=dc, bank=bank, nblk=nblk:
                      e.copy(xTs[:, dc * SEG:dc * SEG + nblk * P],
                             ptr[bank][:, :nblk * P]),
                      waits=[m_t], mile=True)
            done[("ptr_free", bank)] = m_ev
            m_evs.append(m_ev)
        return m_evs

    def emit_h_apply(xv, width, st, m_st, tgt_fn, edge=None):
        narrow = width < 256

        def lve(fn, waits=(), mile=False):
            return vef(fn, waits) if narrow else ve(fn, waits, mile)
        m_last = None
        for dc in range(NCH):
            lve(lambda e, dc=dc: e.tensor_mul(st["sq0"][:, :width], xv(dc),
                                              st["sB"][:, :width]),
                waits=[m_st] if dc == 0 else ())
            lve(lambda e: e.tensor_sub(st["sq0"][:, :width],
                                       st["sq0"][:, :width],
                                       st["msB"][:, :width]))
            if edge is None:
                m_last = lve(lambda e, dc=dc: e.tensor_scalar(
                    tgt_fn(dc), st["sq0"][:, :width], g1[:, dc:dc + 1],
                    b1[:, dc:dc + 1], AL.mult, AL.add),
                    mile=(dc == NCH - 1))
            else:
                colsel, mk, tgt = edge(dc)
                lve(lambda e, dc=dc: e.tensor_scalar(
                    st["sq0"][:, :width], st["sq0"][:, :width],
                    g1[:, dc:dc + 1], b1[:, dc:dc + 1], AL.mult, AL.add))
                m_last = lve(lambda e, tgt=tgt, colsel=colsel, mk=mk:
                             e.tensor_mul(tgt,
                                          st["sq0"][:, colsel:colsel + 1],
                                          mk), mile=(dc == NCH - 1))
        return m_last

    PR.wait("tensor", m_c0)

    for side, r0 in (("L", 0), ("R", TLOC + 2 - P)):
        m_x = ld("s_x0", lambda e, r0=r0: e.dma_start(
            out=xload[0][:, :C], in_=xs[r0:r0 + P, :]),
            waits=[done.get("xload0_free")])
        m_evs = emit_block_transposes(
            lambda tb, dc: xload[0][:, dc * P:(dc + 1) * P], 1,
            [m_x, done.get("xTs_free"), done.get("xTs_free2")])
        done["xload0_free"] = m_evs[-1]

        def xv(dc):
            return xTs[:, dc * SEG:dc * SEG + P]
        m_st = emit_ln_stats(xv, P, st1,
                             m_evs + [done.get("xTs_free")], double=True)

        def mkedge(dc, side=side):
            tgt = (hL if side == "L" else hR)[:, dc:dc + 1]
            return (0 if side == "L" else P - 1,
                    maskl if side == "L" else maskr, tgt)
        m_sl = emit_h_apply(xv, P, st1, m_st, None, mkedge)
        done["h" + side + "s"] = m_sl
        done["xTs_free"] = m_sl

    for _eng in ("tensor", "vector", "scalar", "sync", "gpsimd"):
        PR.wait(_eng, done["hRs"], done["hLs"])

    def stageA(g):
        m_evs = []
        first_wts = [done.get("xTs_free"), done.get("xTs_free2")]
        for tb in range(NTB):
            r0 = 1 + g * SEG + tb * P
            slot = tb % 2
            m_x = ld(f"s_x{slot}", lambda e, r0=r0, slot=slot: e.dma_start(
                out=xload[slot], in_=xs[r0:r0 + P, :]),
                waits=[done.get(("xload_free", slot))])
            m_trl = None
            for half in range(2):
                bank = trp[0] % 2
                trp[0] += 1
                wts = [m_x, done.get(("ptr_free", bank))] + first_wts
                first_wts = []
                for j in range(4):
                    dc = half * 4 + j
                    m_trl = te(lambda e, dc=dc, j=j, slot=slot, bank=bank:
                               e.transpose(ptr[bank][:, j * P:(j + 1) * P],
                                           xload[slot][:, dc * P:
                                                       (dc + 1) * P],
                                           ident_t),
                               waits=wts, mile=(j == 3))
                    wts = []
                m_ev = sc(lambda e, half=half, tb=tb, bank=bank: e.copy(
                    xTs.rearrange("p (dc t) -> p dc t", dc=NCH)[
                        :, half * 4:half * 4 + 4, tb * P:(tb + 1) * P],
                    ptr[bank][:, :].rearrange("p (j t) -> p j t", j=4)),
                    waits=[m_trl], mile=True)
                done[("ptr_free", bank)] = m_ev
                m_evs.append(m_ev)
            done[("xload_free", slot)] = m_trl
        m_xsp = gpd("s_spxT", lambda e, g=g: e.dma_start(
            out=sp_xT.rearrange("p (dc t) -> p dc t", dc=NCH)[
                :, :, g * SEG:(g + 1) * SEG],
            in_=xTs.rearrange("p (dc t) -> p dc t", dc=NCH)),
            waits=[m_evs[-1]])
        done["xTs_free2"] = m_xsp

        def xv(dc):
            return xTs[:, dc * SEG:(dc + 1) * SEG]
        m_st = emit_ln_stats(xv, SEG, st1,
                             m_evs + [done.get("xTs_free")], double=True)

        hbuf = hs[g % 2]

        def tgt(dc):
            return hbuf[:, dc * HSW + 1:dc * HSW + 1 + SEG]
        m_h = emit_h_apply(xv, SEG, st1, m_st, tgt)
        done[("h", g)] = m_h
        done["xTs_free"] = m_h
        m_hsp = gpd("s_sph", lambda e, g=g, hbuf=hbuf: e.dma_start(
            out=sp_h.rearrange("p (dc t) -> p dc t", dc=NCH)[
                :, :, g * SEG:(g + 1) * SEG],
            in_=hbuf.rearrange("p (dc w) -> p dc w", w=HSW)[:, :, 1:1 + SEG]),
            waits=[m_h])
        if "h" in dbg:
            gpd("s_dbg", lambda e, g=g, hbuf=hbuf: e.dma_start(
                out=dbg["h"].rearrange("p (dc t) -> p dc t", dc=NCH)[
                    :, :, g * SEG:(g + 1) * SEG],
                in_=hbuf.rearrange("p (dc w) -> p dc w", w=HSW)[
                    :, :, 1:1 + SEG]), waits=[m_h])
        cur = hbuf.rearrange("p (dc w) -> p dc w", w=HSW)
        if g == 0:
            m_ml = ve(lambda e, cur=cur: e.tensor_copy(
                cur[:, :, 0:1], hL.rearrange("p (dc w) -> p dc w", w=1)),
                waits=[m_h, done["hLs"]], mile=True)
        else:
            prv = hs[(g - 1) % 2].rearrange("p (dc w) -> p dc w", w=HSW)
            m_ml = ve(lambda e, cur=cur, prv=prv: e.tensor_copy(
                cur[:, :, 0:1], prv[:, :, SEG:SEG + 1]),
                waits=[m_h, done[("h", g - 1)]], mile=True)
            m_mr = ve(lambda e, cur=cur, prv=prv: e.tensor_copy(
                prv[:, :, SEG + 1:SEG + 2], cur[:, :, 1:2]), mile=True)
            done[("hmargR", g - 1)] = m_mr
        done[("hmargL", g)] = m_ml
        if g == NSEG - 1:
            m_mr = ve(lambda e, cur=cur: e.tensor_copy(
                cur[:, :, SEG + 1:SEG + 2],
                hR.rearrange("p (dc w) -> p dc w", w=1)),
                waits=[done["hRs"], m_h], mile=True)
            done[("hmargR", g)] = m_mr

    def emit_mix3(hbuf_r, out_fn, mc_coef, m_coef, x_coef, c_coef, waits):
        wts = list(waits)
        m_last = None
        for dc in range(NCH):
            o = out_fn(dc)
            cur = hbuf_r[:, dc, 1:1 + SEG]
            m1 = hbuf_r[:, dc, 0:SEG]
            p1 = hbuf_r[:, dc, 2:2 + SEG]
            if dc < NCH // 2:
                ve(lambda e, o=o, m1=m1, dc=dc: e.tensor_scalar(
                    o, m1, mc_coef[:, dc:dc + 1], None, AL.mult), waits=wts)
                m_last = ve(lambda e, o=o, cur=cur, dc=dc:
                            e.scalar_tensor_tensor(
                                o, cur, x_coef[:, dc:dc + 1], o,
                                AL.mult, AL.add), mile=(dc == NCH - 1))
            else:
                ve(lambda e, o=o, m1=m1, dc=dc: e.tensor_scalar(
                    o, m1, m_coef[:, dc:dc + 1], None, AL.mult), waits=wts)
                ve(lambda e, o=o, cur=cur, dc=dc: e.scalar_tensor_tensor(
                    o, cur, x_coef[:, dc:dc + 1], o, AL.mult, AL.add))
                m_last = ve(lambda e, o=o, p1=p1, dc=dc:
                            e.scalar_tensor_tensor(
                                o, p1, c_coef[:, dc:dc + 1], o,
                                AL.mult, AL.add), mile=(dc == NCH - 1))
            wts = []
        return m_last

    def stageB(g, scan_prev=False):
        hbuf_r = hs[g % 2].rearrange("p (dc w) -> p dc w", w=HSW)
        m_mix = emit_mix3(hbuf_r,
                          lambda dc: xm[:, dc * SEG:(dc + 1) * SEG],
                          OMTMC, OMTM, atm, acm,
                          [done[("hmargL", g)], done[("hmargR", g)],
                           done.get("xm_free")])
        done[("xm", g)] = m_mix
        m_wv0 = load_wv_half(g, 0)
        m_wv1 = load_wv_half(g, 1)
        done[("wv_ld", g)] = (m_wv0, m_wv1)
        if scan_prev:
            scanblock(g - 1)
        if "xm" in dbg:
            gpd("s_dbg", lambda e, g=g: e.dma_start(
                out=dbg["xm"][:, g * NCH * SEG:(g + 1) * NCH * SEG], in_=xm),
                waits=[m_mix])
        PR.wait("tensor", m_wk, m_mix)
        for cc in range(NCH):
            bank = cc % 2
            m_mm = None
            for dc in range(NCH):
                m_mm = te(lambda e, dc=dc, cc=cc, bank=bank: e.matmul(
                    pmm[bank], wtile(wk_t, dc, cc),
                    xm[:, dc * SEG:(dc + 1) * SEG],
                    start=(dc == 0), stop=(dc == NCH - 1)),
                    waits=([done.get(("pmm_free", bank))] if dc == 0 else ()),
                    mile=(dc == NCH - 1))
            k_ = kt2[g % 2][:, cc * SEG:(cc + 1) * SEG]
            m_min = ve(lambda e, k_=k_, bank=bank: e.tensor_scalar(
                k_, pmm[bank], K_CLAMP, None, AL.min), waits=[m_mm],
                mile=True)
            m_ev = sc(lambda e, k_=k_: e.activation(k_, k_, AF.Exp),
                      waits=[m_min], mile=True)
            done[("pmm_free", bank)] = m_ev
            done[("k", g, cc)] = m_ev
        if "k" in dbg:
            m_kdbg = gpd("s_dbg", lambda e, g=g: e.dma_start(
                out=dbg["k"][:, g * NCH * SEG:(g + 1) * NCH * SEG].rearrange(
                    "p (dc t) -> p dc t", dc=NCH),
                in_=kt2[g % 2]),
                waits=[done[("k", g, NCH - 1)]])
            PR.wait("vector", m_kdbg)
        for cc in range(NCH):
            bank = cc % 2
            m_mm = None
            for dc in range(NCH):
                m_mm = te(lambda e, dc=dc, cc=cc, bank=bank: e.matmul(
                    pmm[bank], wv_tile(dc, cc),
                    xm[:, dc * SEG:(dc + 1) * SEG],
                    start=(dc == 0), stop=(dc == NCH - 1)),
                    waits=([done.get(("pmm_free", bank))]
                           if dc == 0 else []) +
                    ([done[("wv_ld", g)][dc // 4]] if dc % 4 == 0 else []),
                    mile=(dc == NCH - 1))
            k_src = kt2[g % 2][:, cc * SEG:(cc + 1) * SEG]
            kv_ = kt2[(g + 1) % 2][:, cc * SEG:(cc + 1) * SEG]
            if "v" in dbg:
                par = cc % 2
                m_vc = ve(lambda e, bank=bank, par=par: e.tensor_copy(
                    dent[par], pmm[bank]),
                    waits=[m_mm, done.get(("dent_free", par))], mile=True)
                m_vd = gpd("s_dbg", lambda e, g=g, cc=cc, par=par:
                           e.dma_start(
                               out=dbg["v"][:, (g * NCH + cc) * SEG:
                                            (g * NCH + cc + 1) * SEG],
                               in_=dent[par]), waits=[m_vc])
                done[("dent_free", par)] = m_vd
            m_kv = ve(lambda e, kv_=kv_, k_src=k_src, bank=bank: e.tensor_mul(
                kv_, k_src, pmm[bank]), waits=[m_mm], mile=True)
            done[("pmm_free", bank)] = m_kv
            done[("kv", g, cc)] = m_kv
        done["xm_free"] = done[("kv", g, NCH - 1)]
        done[("wv_free", 0)] = done[("kv", g, NCH - 1)]
        done[("wv_free", 1)] = done[("kv", g, NCH - 1)]
    def scanblock(g):
        for cc in range(NCH):
            k_ = kt2[g % 2][:, cc * SEG:(cc + 1) * SEG]
            sbv = sb_t[:, cc * SW + 1:(cc + 1) * SW]
            sbsh = sb_t[:, cc * SW:(cc + 1) * SW - 1]
            par = cc % 2
            PR.wait("vector", done[("k", g, cc)])
            m_scn = ve(lambda e, sbv=sbv, k_=k_, cc=cc: e.tensor_tensor_scan(
                sbv, ED[:, cc:cc + 1].broadcast_to((P, SEG)), k_, 0.0,
                AL.mult, AL.add), mile=True)
            ve(lambda e, sbv=sbv, cc=cc, g=g: e.tensor_copy(
                Ab[:, cc * NSEG + g:cc * NSEG + g + 1], sbv[:, SEG - 1:SEG]),
               waits=[m_scn])
            m_den = ve(lambda e, k_=k_, sbsh=sbsh, cc=cc, par=par:
                       e.scalar_tensor_tensor(dent[par], k_,
                                              EU[:, cc:cc + 1], sbsh,
                                              AL.mult, AL.add),
                       waits=[done.get(("dent_free", par))], mile=True)
            m_dsp = gpd(f"s_spd{par}", lambda e, g=g, cc=cc, par=par:
                        e.dma_start(
                            out=sp_den[:, (g * NCH + cc) * SEG:
                                       (g * NCH + cc + 1) * SEG],
                            in_=dent[par]), waits=[m_den])
            if "den" in dbg:
                m_dsp = gpd(f"s_spd{par}", lambda e, g=g, cc=cc, par=par:
                            e.dma_start(
                                out=dbg["den"][:, (g * NCH + cc) * SEG:
                                               (g * NCH + cc + 1) * SEG],
                                in_=dent[par]), waits=[m_den])
            done[("dent_free", par)] = m_dsp
        for cc in range(NCH):
            kv_ = kt2[(g + 1) % 2][:, cc * SEG:(cc + 1) * SEG]
            sav = sb_t[:, cc * SW + 1:(cc + 1) * SW]
            sash = sb_t[:, cc * SW:(cc + 1) * SW - 1]
            par = cc % 2
            PR.wait("vector", done[("kv", g, cc)])
            m_scn = ve(lambda e, sav=sav, kv_=kv_, cc=cc: e.tensor_tensor_scan(
                sav, ED[:, cc:cc + 1].broadcast_to((P, SEG)), kv_, 0.0,
                AL.mult, AL.add), mile=True)
            ve(lambda e, sav=sav, cc=cc, g=g: e.tensor_copy(
                Aa[:, cc * NSEG + g:cc * NSEG + g + 1], sav[:, SEG - 1:SEG]),
               waits=[m_scn])
            m_num = ve(lambda e, kv_=kv_, sash=sash, cc=cc, par=par:
                       e.scalar_tensor_tensor(numt[par], kv_,
                                              EU[:, cc:cc + 1], sash,
                                              AL.mult, AL.add),
                       waits=[done.get(("numt_free", par))], mile=True)
            m_nsp = gpd(f"s_spn{par}", lambda e, g=g, cc=cc, par=par:
                        e.dma_start(
                            out=sp_num[:, (g * NCH + cc) * SEG:
                                       (g * NCH + cc + 1) * SEG],
                            in_=numt[par]), waits=[m_num])
            if "num" in dbg:
                m_nsp = gpd(f"s_spn{par}", lambda e, g=g, cc=cc, par=par:
                            e.dma_start(
                                out=dbg["num"][:, (g * NCH + cc) * SEG:
                                               (g * NCH + cc + 1) * SEG],
                                in_=numt[par]), waits=[m_num])
            done[("numt_free", par)] = m_nsp

    stageA(0)
    stageA(1)
    stageB(0)
    for g in range(1, NSEG):
        if g + 1 < NSEG:
            stageA(g + 1)
        stageB(g, scan_prev=True)
    scanblock(NSEG - 1)
    AR.pop()

    p1_nd_done = [sem("s_spn0").total(), sem("s_spn1").total(),
                  sem("s_spd0").total(), sem("s_spd1").total()]
    p1_h_done = sem("s_sph").total()
    p1_xT_done = sem("s_spxT").total()

    def emit_cx_b():
        agc = AG2.rearrange("p (c r) -> p c r", r=NCORES)
        selbc = selprev.rearrange("p (o r) -> p o r", o=1).broadcast_to(
            (P, 2 * NCH, NCORES))
        vef(lambda e: e.tensor_copy(
            agc, agb.rearrange("p (r c) -> p c r", r=NCORES)),
            waits=[m_ag1l])
        vef(lambda e: e.tensor_tensor_scan(
            AG2, D0r, AG2, 0.0, AL.mult, AL.add))
        vef(lambda e: e.tensor_mul(agc, agc, selbc))
        vef(lambda e: e.tensor_add(agc[:, :, 0:4], agc[:, :, 0:4],
                                   agc[:, :, 4:8]))
        vef(lambda e: e.tensor_add(agc[:, :, 0:2], agc[:, :, 0:2],
                                   agc[:, :, 2:4]))
        vef(lambda e: e.tensor_add(
            scC[:, 0:2 * NCH].rearrange("p (c o) -> p c o", o=1),
            agc[:, :, 0:1], agc[:, :, 1:2]))
        m_carry = None
        for (car, iav, col, toff) in ((carry_a, Ia_a, 0, 64),
                                      (carry_b, Ia_b, NCH, 96)):
            carr = car.rearrange("p (dc g) -> p dc g", g=NSEG)
            iar = iav.rearrange("p (dc g) -> p dc g", g=NSEG)
            tmp = agb[:, toff:toff + NCH * NSEG]
            tmpr = tmp.rearrange("p (dc g) -> p dc g", g=NSEG)
            scCexp = scC[:, col:col + NCH].rearrange(
                "p (dc o) -> p dc o", o=1).broadcast_to((P, NCH, NSEG))
            vef(lambda e, carr=carr: e.memset(carr[:, :, 0:1], 0.0))
            vef(lambda e, carr=carr, iar=iar: e.tensor_copy(
                carr[:, :, 1:NSEG], iar[:, :, 0:NSEG - 1]))
            vef(lambda e, tmpr=tmpr, scCexp=scCexp: e.tensor_mul(
                tmpr, powg.rearrange("p (dc g) -> p dc g", g=NSEG),
                scCexp))
            m_carry = vef(lambda e, car=car, tmp=tmp: e.tensor_add(
                car, car, tmp))
        if "carry" in dbg:
            gpd("s_dbg", lambda e: e.dma_start(
                out=dbg["carry"][:, 0:NCH * NSEG], in_=carry_a), waits=[m_carry])
            gpd("s_dbg", lambda e: e.dma_start(
                out=dbg["carry"][:, NCH * NSEG:2 * NCH * NSEG], in_=carry_b),
                waits=[m_carry])
        if "cx" in dbg:
            gpd("s_dbg", lambda e: e.dma_start(
                out=dbg["cx"][:, 0:32], in_=Ia_a), waits=[m_carry])
            gpd("s_dbg", lambda e: e.dma_start(
                out=dbg["cx"][:, 32:64], in_=Ia_b), waits=[m_carry])
            gpd("s_dbg", lambda e: e.dma_start(
                out=dbg["cx"][:, 64:80], in_=scA[:, 0:16]), waits=[m_carry])
            gpd("s_dbg", lambda e: e.dma_start(
                out=dbg["cx"][:, 80:96], in_=scC[:, 0:16]), waits=[m_carry])

            gpd("s_dbg", lambda e: e.dma_start(
                out=dbg["cx"][:, 224:256], in_=powg), waits=[m_carry])

        done["m_carry"] = m_carry

    AR.push()
    wr_t = AR.alloc(NCH * C)
    wo_t = AR.alloc(NCH * C)
    hs2 = [AR.alloc(NCH * HSW) for _ in range(2)]
    xm2 = AR.alloc(NCH * SEG)
    powt = AR.alloc(NCH * SEG)
    numb = AR.alloc(NCH * SEG)
    denb = AR.alloc(NCH * SEG)
    st2 = alloc_stats()
    sigc = [AR.alloc(SEG) for _ in range(2)]
    recc = [AR.alloc(SEG) for _ in range(2)]
    wst = [AR.alloc(SEG) for _ in range(2)]
    sig8 = AR.alloc(NCH * SEG)

    p1_done = done[("kv", NSEG - 1, NCH - 1)]
    m_wr = ld("s_wro", lambda e: e.dma_start(
        out=wr_t.rearrange("p (dc c) -> p dc c", dc=NCH),
        in_=wrT.rearrange("(dc p) c -> p dc c", p=P)),
        waits=[p1_done] + p1_nd_done)
    m_wo = ld("s_wro", lambda e: e.dma_start(
        out=wo_t.rearrange("p (dc c) -> p dc c", dc=NCH),
        in_=woT.rearrange("(dc p) c -> p dc c", p=P)),
        waits=[p1_done] + p1_nd_done)

    D0a = AR.alloc(NCH * NSEG)
    D0r = AR.alloc(2 * NCH * NCORES)
    AG2 = AR.alloc(2 * NCH * NCORES)
    D0ar = D0a.rearrange("p (dc g) -> p dc g", g=NSEG)
    D0rr = D0r.rearrange("p (c r) -> p c r", r=NCORES)
    vef(lambda e: e.tensor_copy(
        D0ar, EDT512.rearrange("p (dc o) -> p dc o", o=1).broadcast_to(
            (P, NCH, NSEG))), waits=[p1_done] + p1_nd_done)
    vef(lambda e: e.memset(D0ar[:, :, 0:1], 0.0))
    vef(lambda e: e.tensor_copy(
        D0rr[:, 0:NCH, :],
        EDT2048.rearrange("p (dc o) -> p dc o", o=1).broadcast_to(
            (P, NCH, NCORES))))
    vef(lambda e: e.tensor_copy(
        D0rr[:, NCH:2 * NCH, :],
        EDT2048.rearrange("p (dc o) -> p dc o", o=1).broadcast_to(
            (P, NCH, NCORES))))
    vef(lambda e: e.memset(D0rr[:, :, 0:1], 0.0))
    vef(lambda e: e.tensor_tensor_scan(Ia_a, D0a, Aa, 0.0,
                                       AL.mult, AL.add))
    vef(lambda e: e.tensor_tensor_scan(Ia_b, D0a, Ab, 0.0,
                                       AL.mult, AL.add))
    vef(lambda e: e.tensor_copy(
        scA[:, 0:NCH].rearrange("p (c o) -> p c o", o=1),
        Ia_a.rearrange("p (dc g) -> p dc g", g=NSEG)[:, :, NSEG - 1:NSEG]))
    m_acore = vef(lambda e: e.tensor_copy(
        scA[:, NCH:2 * NCH].rearrange("p (c o) -> p c o", o=1),
        Ia_b.rearrange("p (dc g) -> p dc g", g=NSEG)[:, :, NSEG - 1:NSEG]))
    m_ag1s = gpd("s_ag", lambda e: e.dma_start(out=ag1_in,
                                               in_=scA[:, :2 * NCH]),
                 waits=[m_acore])
    m_cc1 = gpc(lambda e: e.collective_compute(
        "AllGather", AL.bypass, replica_groups=[list(range(NCORES))],
        ins=[ag1_in.opt()], outs=[ag1_out.opt()]), waits=[m_ag1s])
    m_ag1l = gpd("s_ag", lambda e: e.dma_start(
        out=agb.rearrange("p (r c) -> p r c", r=NCORES),
        in_=ag1_out.rearrange("(r p) c -> p r c", p=P)), waits=[m_cc1])

    def emit_powt():
        PR.wait("vector", done["m_carry"])
        edp = scB
        vef(lambda e: e.tensor_copy(edp[:, 0:NCH], ED))
        powtr = powt.rearrange("p (dc t) -> p dc t", t=SEG)
        vef(lambda e: e.memset(powtr[:, :, 0:1], 1.0))
        w_ = 1
        while w_ < SEG:
            edbc = edp[:, 0:NCH].rearrange(
                "p (dc o) -> p dc o", o=1).broadcast_to((P, NCH, w_))
            vef(lambda e, w_=w_, edbc=edbc: e.tensor_mul(
                powtr[:, :, w_:2 * w_], powtr[:, :, 0:w_], edbc))
            if 2 * w_ < SEG:
                vef(lambda e: e.tensor_mul(edp[:, 0:NCH], edp[:, 0:NCH],
                                           edp[:, 0:NCH]))
            w_ *= 2
        done["m_powt"] = vef(lambda e: e.tensor_copy(scD[:, 0:1],
                                                     powt[:, 0:1]))

    def load_h2(g, slot):
        lo = g * SEG - 1
        dst = hs2[slot].rearrange("p (dc w) -> p dc w", w=HSW)
        sph = sp_h.rearrange("p (dc t) -> p dc t", dc=NCH)
        wts = [done.get(("hs2_free", slot)), p1_h_done, p1_xT_done] + \
            p1_nd_done
        if g == 0:
            m = ld("s_h2", lambda e, dst=dst, sph=sph: e.dma_start(
                out=dst[:, :, 1:HSW], in_=sph[:, :, 0:SEG + 1]), waits=wts)
            m2 = ve(lambda e, dst=dst: e.tensor_copy(
                dst[:, :, 0:1], hL.rearrange("p (dc w) -> p dc w", w=1)),
                waits=[m], mile=True)
        elif g == NSEG - 1:
            m = ld("s_h2", lambda e, dst=dst, sph=sph, lo=lo: e.dma_start(
                out=dst[:, :, 0:HSW - 1], in_=sph[:, :, lo:lo + SEG + 1]),
                waits=wts)
            m2 = ve(lambda e, dst=dst: e.tensor_copy(
                dst[:, :, HSW - 1:HSW],
                hR.rearrange("p (dc w) -> p dc w", w=1)),
                waits=[m], mile=True)
        else:
            m = ld("s_h2", lambda e, dst=dst, sph=sph, lo=lo: e.dma_start(
                out=dst, in_=sph[:, :, lo:lo + HSW]), waits=wts)
            m2 = ve(lambda e: e.engine_nop(), waits=[m], mile=True)
        done[("h2", g)] = m2

    def emit_mix2(g):
        hbuf_r = hs2[g % 2].rearrange("p (dc w) -> p dc w", w=HSW)
        m_mix = emit_mix3(hbuf_r,
                          lambda dc: xm2[:, dc * SEG:(dc + 1) * SEG],
                          OMTMC, OMTM, atm, acm,
                          [done[("h2", g)], done.get("xm2_free")])
        done[("mix2", g)] = m_mix

    load_h2(0, 0)
    emit_mix2(0)
    for g in range(NSEG):
        hbuf_r = hs2[g % 2].rearrange("p (dc w) -> p dc w", w=HSW)
        if g + 1 < NSEG:
            load_h2(g + 1, (g + 1) % 2)
        m_nl = ld("s_nb", lambda e, g=g: e.dma_start(
            out=numb, in_=sp_num[:, g * NCH * SEG:(g + 1) * NCH * SEG]),
            waits=[done.get("numb_free")] + p1_nd_done)
        m_dl = ld("s_db", lambda e, g=g: e.dma_start(
            out=denb, in_=sp_den[:, g * NCH * SEG:(g + 1) * NCH * SEG]),
            waits=[done.get("denb_free")] + p1_nd_done)
        PR.wait("tensor", m_wr, m_wo, done[("mix2", g)])
        m_rmm_last = None
        for cc in range(NCH):
            bank = cc % 2
            par = cc % 2
            m_mm = None
            for dc in range(NCH):
                m_mm = te(lambda e, dc=dc, cc=cc, bank=bank: e.matmul(
                    pmm[bank], wtile(wr_t, dc, cc),
                    xm2[:, dc * SEG:(dc + 1) * SEG],
                    start=(dc == 0), stop=(dc == NCH - 1)),
                    waits=([done.get(("pmm_free", bank))] if dc == 0
                           else ()), mile=(dc == NCH - 1))
            m_rmm_last = m_mm
            m_e = sc(lambda e, bank=bank, cc=cc: e.activation(
                sig8[:, cc * SEG:(cc + 1) * SEG], pmm[bank], AF.Sigmoid),
                waits=[m_mm], mile=True)
            done[("pmm_free", bank)] = m_e
            done[("sig2", g, cc)] = m_e
        done["xm2_free"] = m_rmm_last
        if g == 0:
            emit_cx_b()
            emit_powt()
            if "cx" in dbg:
                gpd("s_dbg", lambda e: e.dma_start(
                    out=dbg["cx"][:, 96:224], in_=powt[:, 0:128]),
                    waits=[done["m_powt"]])
        for cc in range(NCH):
            par = cc % 2
            nv = numb[:, cc * SEG:(cc + 1) * SEG]
            dv = denb[:, cc * SEG:(cc + 1) * SEG]
            pt_ = powt[:, cc * SEG:(cc + 1) * SEG]
            PR.wait("vector", m_nl, m_dl, done["m_powt"], done["m_carry"])
            ve(lambda e, nv=nv, pt_=pt_, cc=cc, g=g: e.scalar_tensor_tensor(
                nv, pt_, carry_a[:, cc * NSEG + g:cc * NSEG + g + 1], nv,
                AL.mult, AL.add))
            ve(lambda e, dv=dv, pt_=pt_, cc=cc, g=g: e.scalar_tensor_tensor(
                dv, pt_, carry_b[:, cc * NSEG + g:cc * NSEG + g + 1], dv,
                AL.mult, AL.add))
            m_rd = ve(lambda e, dv=dv, par=par: e.reciprocal(recc[par], dv),
                      waits=[done.get(("recc_free", par))], mile=True)
            if "y" in dbg:
                m_yd = ve(lambda e, nv=nv, par=par: e.tensor_mul(
                    nv, nv, recc[par]), waits=[m_rd], mile=True)
                m_ydd = gpd("s_dbg", lambda e, g=g, cc=cc, nv=nv:
                            e.dma_start(
                                out=dbg["y"][:, (g * NCH + cc) * SEG:
                                             (g * NCH + cc + 1) * SEG],
                                in_=nv), waits=[m_yd])
                m_p = ve(lambda e, nv=nv, cc=cc: e.tensor_mul(
                    nv, nv, sig8[:, cc * SEG:(cc + 1) * SEG]),
                    waits=[m_ydd, done[("sig2", g, cc)]], mile=True)
            else:
                ve(lambda e, nv=nv, par=par: e.tensor_mul(nv, nv,
                                                          recc[par]),
                   waits=[m_rd])
                m_p = ve(lambda e, nv=nv, cc=cc: e.tensor_mul(
                    nv, nv, sig8[:, cc * SEG:(cc + 1) * SEG]),
                    waits=[done[("sig2", g, cc)]], mile=True)
            done[("recc_free", par)] = m_p
            done[("p", g, cc)] = m_p
        PR.wait("tensor", *[done[("p", g, cc)] for cc in range(NCH)])
        for co in range(NCH):
            bank = co % 2
            m_mm = None
            for ci in range(NCH):
                m_mm = te(lambda e, ci=ci, co=co, bank=bank: e.matmul(
                    pmm[bank], wtile(wo_t, ci, co),
                    numb[:, ci * SEG:(ci + 1) * SEG],
                    start=(ci == 0), stop=(ci == NCH - 1)),
                    waits=([done.get(("pmm_free", bank))] if ci == 0
                           else ()), mile=(ci == NCH - 1))
            uv = denb[:, co * SEG:(co + 1) * SEG]
            hv = hbuf_r[:, co, 1:1 + SEG]
            m_u = ve(lambda e, uv=uv, hv=hv, bank=bank: e.tensor_add(
                uv, hv, pmm[bank]), waits=[m_mm], mile=True)
            done[("pmm_free", bank)] = m_u
        m_ulast = done[("pmm_free", (NCH - 1) % 2)]
        done["numb_free"] = m_ulast
        done[("hs2_free", g % 2)] = m_ulast
        if "u" in dbg:
            gpd("s_dbg", lambda e, g=g: e.dma_start(
                out=dbg["u"][:, g * NCH * SEG:(g + 1) * NCH * SEG],
                in_=denb), waits=[m_ulast])
        denb_r = denb.rearrange("p (cc t) -> p cc t", t=SEG)
        if g == 0:
            m_uf = ve(lambda e, denb_r=denb_r: e.tensor_copy(
                uF.rearrange("p (dc w) -> p dc w", w=1), denb_r[:, :, 0:1]),
                waits=[m_ulast], mile=True)
            m_s = gpd("s_ag", lambda e: e.dma_start(out=ag2a_in, in_=uF),
                      waits=[m_uf])
            done["cc2a"] = gpc(lambda e: e.collective_compute(
                "AllGather", AL.bypass,
                replica_groups=[list(range(NCORES))],
                ins=[ag2a_in.opt()], outs=[ag2a_out.opt()]), waits=[m_s])
        if g == NSEG - 1:
            m_ul = ve(lambda e, denb_r=denb_r: e.tensor_copy(
                uL.rearrange("p (dc w) -> p dc w", w=1),
                denb_r[:, :, SEG - 1:SEG]), waits=[m_ulast], mile=True)
            m_s = gpd("s_ag", lambda e: e.dma_start(out=ag2b_in, in_=uL),
                      waits=[m_ul])
            done["cc2b"] = gpc(lambda e: e.collective_compute(
                "AllGather", AL.bypass,
                replica_groups=[list(range(NCORES))],
                ins=[ag2b_in.opt()], outs=[ag2b_out.opt()]), waits=[m_s])
        if g + 1 < NSEG:
            emit_mix2(g + 1)

        def uv_(dc):
            return denb[:, dc * SEG:(dc + 1) * SEG]
        m_st = emit_ln_stats(uv_, SEG, st2, [m_ulast])
        for cc in range(NCH):
            par = cc % 2
            ve(lambda e, cc=cc: e.tensor_mul(st2["sq0"][:, :SEG], uv_(cc),
                                             st2["sB"][:, :SEG]),
               waits=[m_st] if cc == 0 else ())
            ve(lambda e: e.tensor_sub(st2["sq0"][:, :SEG],
                                      st2["sq0"][:, :SEG],
                                      st2["msB"][:, :SEG]))
            m_w = ve(lambda e, cc=cc, par=par: e.tensor_scalar(
                wst[par], st2["sq0"][:, :SEG], g2[:, cc:cc + 1],
                b2[:, cc:cc + 1], AL.mult, AL.add),
                waits=[done.get(("wst_free", par))], mile=True)
            m_wsp = gpd(f"s_spw{par}", lambda e, g=g, cc=cc, par=par:
                        e.dma_start(
                            out=sp_w.rearrange("p (cc t) -> p cc t",
                                               cc=NCH)[:, cc,
                                                       g * SEG:(g + 1) * SEG],
                            in_=wst[par]), waits=[m_w])
            if "w" in dbg:
                m_wsp = gpd(f"s_spw{par}", lambda e, g=g, cc=cc, par=par:
                            e.dma_start(
                                out=dbg["w"].rearrange(
                                    "p (cc t) -> p cc t", cc=NCH)[
                                    :, cc, g * SEG:(g + 1) * SEG],
                                in_=wst[par]), waits=[m_w])
            done[("wst_free", par)] = m_wsp
        done["denb_free"] = done[("wst_free", (NCH - 1) % 2)]
    AR.pop()
    p2_w_done = [sem("s_spw0").total(), sem("s_spw1").total()]

    AR.push()
    xmch = AR.alloc16(NCH * TLOC)
    xmcl = AR.alloc16(NCH * TLOC)
    cmac = AR.alloc(NCH * TLOC)

    AR.push()
    wroll = [AR.alloc(NCH * HSW) for _ in range(2)]
    xmcs = AR.alloc(NCH * SEG)

    def emit_sel(ag_out, sel, dst, m_ccx, extra=()):
        m_l = gpd("s_ag", lambda e, ag_out=ag_out: e.dma_start(
            out=agb.rearrange("p (r c) -> p r c", r=NCORES)[:, :, 0:NCH],
            in_=ag_out.rearrange("(r p) c -> p r c", p=P)),
            waits=[m_ccx] + list(extra))
        mm = None
        for dc in range(NCH):
            vef(lambda e, dc=dc: e.tensor_mul(
                scB[:, 0:NCORES],
                agb.rearrange("p (r c) -> p r c", r=NCORES)[:, :, dc], sel),
                waits=[m_l] if dc == 0 else ())
            mm = vef(lambda e, dc=dc, dst=dst: e.reduce_sum(
                dst[:, dc:dc + 1], scB[:, 0:NCORES],
                axis=mybir.AxisListType.X))
        return mm

    m_up = emit_sel(ag2b_out, selprev, scC[:, 0:NCH], done["cc2b"])
    m_un = emit_sel(ag2a_out, selnext, scD[:, 0:NCH], done["cc2a"],
                    extra=[m_up])

    def emit_onecol_ln(src, dst, mk, m_src, gv=None, bv=None, extra=()):
        vef(lambda e, src=src: e.reduce_sum(scB[:, 0:1], src,
                                            axis=mybir.AxisListType.X),
            waits=[m_src] + list(extra))
        m_sq = scf(lambda e, src=src: e.square(scB[:, 2:2 + NCH], src),
                   waits=[m_src] + list(extra))
        m_r = vef(lambda e: e.reduce_sum(scB[:, 1:2], scB[:, 2:2 + NCH],
                                         axis=mybir.AxisListType.X),
                  waits=[m_sq])
        m_b = te(lambda e: e.matmul(pst[0][:, 0:2], ones_t, scB[:, 0:2],
                                    start=True, stop=True),
                 waits=[m_r, done.get("pst0_free")], mile=True)
        m0 = vef(lambda e: e.tensor_scalar(scB[:, 0:2], pst[0][:, 0:2],
                                           1.0 / C, None, AL.mult),
                 waits=[m_b])
        done["pst0_free"] = m0
        vef(lambda e: e.tensor_mul(scB[:, 2:3], scB[:, 0:1], scB[:, 0:1]))
        vef(lambda e: e.tensor_sub(scB[:, 1:2], scB[:, 1:2], scB[:, 2:3]))
        m_t = vef(lambda e: e.tensor_scalar(scB[:, 1:2], scB[:, 1:2], 1.0,
                                            LN_EPS, AL.mult, AL.add))
        scf(lambda e: e.activation(scB[:, 2:3], scB[:, 1:2], AF.Ln),
            waits=[m_t])
        m_e = scf(lambda e: e.activation(scB[:, 3:4], scB[:, 2:3], AF.Exp,
                                         scale=-0.5))
        vef(lambda e: e.tensor_mul(scB[:, 2:3], scB[:, 3:4], scB[:, 3:4]),
            waits=[m_e])
        vef(lambda e: e.tensor_mul(scB[:, 2:3], scB[:, 2:3], scB[:, 1:2]))
        vef(lambda e: e.tensor_scalar(scB[:, 2:3], scB[:, 2:3], -0.5, 1.5,
                                      AL.mult, AL.add))
        vef(lambda e: e.tensor_mul(scB[:, 3:4], scB[:, 3:4], scB[:, 2:3]))
        mm = None
        for dc in range(NCH):
            vef(lambda e, dc=dc, src=src: e.tensor_sub(
                scB[:, 4:5], src[:, dc:dc + 1], scB[:, 0:1]))
            vef(lambda e: e.tensor_mul(scB[:, 4:5], scB[:, 4:5],
                                       scB[:, 3:4]))
            if gv is not None:
                vef(lambda e, dc=dc: e.tensor_scalar(
                    scB[:, 4:5], scB[:, 4:5], gv[:, dc:dc + 1],
                    bv[:, dc:dc + 1], AL.mult, AL.add))
            if mk is not None:
                mm = vef(lambda e, dc=dc, dst=dst, mk=mk: e.tensor_mul(
                    dst[:, dc:dc + 1], scB[:, 4:5], mk))
            else:
                mm = vef(lambda e, dc=dc, dst=dst: e.tensor_copy(
                    dst[:, dc:dc + 1], scB[:, 4:5]))
        return mm

    m_wls = emit_onecol_ln(scC[:, 0:NCH], wLs, maskl, m_up, g2, b2,
                           extra=[m_un])
    if "slv" in dbg:
        m_wls = vef(lambda e: e.tensor_copy(agb[:, 6 * NCH:6 * NCH + 8],
                                            scB[:, 0:8]), waits=[m_wls])
    m_wrs = emit_onecol_ln(scD[:, 0:NCH], wRs, maskr, m_un, g2, b2,
                           extra=[m_wls])
    if "slv" in dbg:
        m_sv = vef(lambda e: e.tensor_copy(agb[:, 0:NCH], scC[:, 0:NCH]),
                   waits=[m_wls, m_wrs])
        m_sv = vef(lambda e: e.tensor_copy(agb[:, NCH:2 * NCH],
                                           scD[:, 0:NCH]))
        m_sv = vef(lambda e: e.tensor_copy(agb[:, 2 * NCH:3 * NCH], wLs))
        m_sv = vef(lambda e: e.tensor_copy(agb[:, 3 * NCH:4 * NCH], wRs))
        m_sv = vef(lambda e: e.tensor_copy(agb[:, 4 * NCH:5 * NCH], hL))
        m_sv = vef(lambda e: e.tensor_copy(agb[:, 5 * NCH:6 * NCH], hR))
        gpd("s_dbg", lambda e: e.dma_start(
            out=dbg["slv"], in_=agb[:, 0:7 * NCH]), waits=[m_sv])
        PR.wait("gpsimd", m_sv)

    def load_w3(g, slot):
        lo = g * SEG - 1
        dst = wroll[slot].rearrange("p (dc w) -> p dc w", w=HSW)
        spw = sp_w.rearrange("p (cc t) -> p cc t", cc=NCH)
        wts = [done.get(("wroll_free", slot))] + p2_w_done
        if g == 0:
            m = ld("s_w3", lambda e, dst=dst, spw=spw: e.dma_start(
                out=dst[:, :, 1:HSW], in_=spw[:, :, 0:SEG + 1]), waits=wts)
            m2 = ve(lambda e, dst=dst: e.tensor_copy(
                dst[:, :, 0:1], wLs.rearrange("p (dc w) -> p dc w", w=1)),
                waits=[m, m_wls], mile=True)
        elif g == NSEG - 1:
            m = ld("s_w3", lambda e, dst=dst, spw=spw, lo=lo: e.dma_start(
                out=dst[:, :, 0:HSW - 1], in_=spw[:, :, lo:lo + SEG + 1]),
                waits=wts)
            m2 = ve(lambda e, dst=dst: e.tensor_copy(
                dst[:, :, HSW - 1:HSW],
                wRs.rearrange("p (dc w) -> p dc w", w=1)),
                waits=[m, m_wrs], mile=True)
        else:
            m = ld("s_w3", lambda e, dst=dst, spw=spw, lo=lo: e.dma_start(
                out=dst, in_=spw[:, :, lo:lo + HSW]), waits=wts)
            m2 = ve(lambda e: e.engine_nop(), waits=[m], mile=True)
        done[("w3", g)] = m2

    load_w3(0, 0)
    xmchr = xmch.rearrange("p (dc t) -> p dc t", dc=NCH)
    xmclr = xmcl.rearrange("p (dc t) -> p dc t", dc=NCH)
    for g in range(NSEG):
        if g + 1 < NSEG:
            load_w3(g + 1, (g + 1) % 2)
        wbuf_r = wroll[g % 2].rearrange("p (dc w) -> p dc w", w=HSW)
        m_mixc = emit_mix3(
            wbuf_r, lambda dc: xmcs[:, dc * SEG:(dc + 1) * SEG],
            OMTMFC, OMTMF, ftm, fcm,
            [done[("w3", g)], done.get("xmcs_free"),
             done.get("xmcs_free2")])
        done[("wroll_free", g % 2)] = m_mixc
        if "xmc" in dbg:
            done["xmcs_free2"] = gpd(
                "s_dbg", lambda e, g=g: e.dma_start(
                    out=dbg["xmc"].rearrange("p (dc t) -> p dc t",
                                             dc=NCH)[:, :,
                                                     g * SEG:(g + 1) * SEG],
                    in_=xmcs.rearrange("p (dc t) -> p dc t", dc=NCH)),
                waits=[m_mixc])
        m_cast = None
        for dc in range(NCH):
            m_cast = sc(lambda e, dc=dc, g=g: e.copy(
                xmchr[:, dc, g * SEG:(g + 1) * SEG],
                xmcs[:, dc * SEG:(dc + 1) * SEG]),
                waits=[m_mixc] if dc == 0 else (), mile=(dc == NCH - 1))
        done["xmcs_free"] = m_cast
        m_sub = None
        for dc in range(NCH):
            m_sub = ve(lambda e, dc=dc, g=g: e.tensor_sub(
                xmclr[:, dc, g * SEG:(g + 1) * SEG],
                xmcs[:, dc * SEG:(dc + 1) * SEG],
                xmchr[:, dc, g * SEG:(g + 1) * SEG]),
                waits=[m_cast] if dc == 0 else (), mile=(dc == NCH - 1))
        done[("xmc", g)] = m_sub
    AR.pop()

    AR.push()
    fkp = [AR.alloc16(2 * NCH * SEG) for _ in range(2)]
    fvp = [AR.alloc16(2 * 4 * C) for _ in range(2)]
    stg3 = AR.alloc(SEG)
    kt = AR.alloc(SEG)
    ksqh = AR.alloc16(4 * SEG)
    ksql = AR.alloc16(4 * SEG)

    def fk_hi(slot):
        return fkp[slot][:, 0:NCH * SEG]

    def fk_lo(slot):
        return fkp[slot][:, NCH * SEG:2 * NCH * SEG]

    def fv_hi(slot):
        return fvp[slot][:, 0:4 * C]

    def fv_lo(slot):
        return fvp[slot][:, 4 * C:8 * C]

    def make_chunk_emitter(hsl):
        slot = hsl % 2
        state = {"ci": 0}
        fkTr = fkT.rearrange("(dc p) hh -> p dc hh", p=P)
        fvTr = fvT[hsl * SEG:(hsl + 1) * SEG, :].rearrange(
            "(hc p) c -> p hc c", p=P)

        def emit(n):
            for _ in range(n):
                ci = state["ci"]
                if ci >= 16:
                    return
                if ci < 8:
                    src = fkTr[:, ci, hsl * SEG:(hsl + 1) * SEG]
                    dhi = fk_hi(slot)[:, ci * SEG:(ci + 1) * SEG]
                    dlo = fk_lo(slot)[:, ci * SEG:(ci + 1) * SEG]
                else:
                    hc, half = divmod(ci - 8, 2)
                    off = hc * C + half * SEG
                    src = fvTr[:, hc, half * SEG:(half + 1) * SEG]
                    dhi = fv_hi(slot)[:, off:off + SEG]
                    dlo = fv_lo(slot)[:, off:off + SEG]
                wts = [done.get("chunk_prev")]
                if ci == 0:
                    wts += [done.get(("slab_free", slot)),
                            done.get("denb_free"),
                            done[("xmc", NSEG - 1)]] + p2_w_done
                m_l = ld(f"s_fk{slot}", lambda e, src=src: e.dma_start(
                    out=stg3, in_=src), waits=wts)
                m_c = sc(lambda e, dhi=dhi: e.copy(dhi, stg3),
                         waits=[m_l], mile=True)
                done["chunk_prev"] = ve(
                    lambda e, dlo=dlo, dhi=dhi: e.tensor_sub(
                        dlo, stg3, dhi), waits=[m_c], mile=True)
                if ci == 7:
                    done[("fkdone", hsl)] = done["chunk_prev"]
                elif ci == 15:
                    done[("fvdone", hsl)] = done["chunk_prev"]
                state["ci"] = ci + 1
        return emit

    make_chunk_emitter(0)(16)
    for hsl in range(NSLAB):
        slot = hsl % 2
        em_next = make_chunk_emitter(hsl + 1) if hsl + 1 < NSLAB else None
        for g in range(NSEG):
            PR.wait("tensor", done[("fkdone", hsl)], done[("xmc", g)])
            for hc in range(4):
                bank = hc % 2
                m_mm = None
                for pi in range(3):
                    wtile_ = (fk_hi if pi < 2 else fk_lo)(slot)
                    xr_ = xmchr if pi != 1 else xmclr
                    for dc in range(NCH):
                        first = (pi == 0 and dc == 0)
                        last = (pi == 2 and dc == NCH - 1)
                        m_mm = te(lambda e, dc=dc, hc=hc, g=g, bank=bank,
                                  wtile_=wtile_, xr_=xr_, first=first,
                                  last=last: e.matmul(
                                      pmm[bank],
                                      wtile_[:, dc * SEG + hc * P:
                                             dc * SEG + (hc + 1) * P],
                                      xr_[:, dc, g * SEG:(g + 1) * SEG],
                                      start=first, stop=last),
                                  waits=([done.get(("pmm_free", bank))]
                                         if first else ()),
                                  mile=last)
                m_rl = sc(lambda e, bank=bank: e.activation(
                    kt, pmm[bank], AF.Relu),
                    waits=[m_mm, done.get("kt_free")], mile=True)
                done[("pmm_free", bank)] = m_rl
                m_sq = ve(lambda e: e.tensor_mul(kt, kt, kt),
                          waits=[m_rl], mile=True)
                kh = ksqh[:, hc * SEG:(hc + 1) * SEG]
                klo = ksql[:, hc * SEG:(hc + 1) * SEG]
                m_ch = sc(lambda e, kh=kh: e.copy(kh, kt),
                          waits=[m_sq, done.get(("ksq_free", hc))],
                          mile=True)
                m_sb = ve(lambda e, klo=klo, kh=kh: e.tensor_sub(
                    klo, kt, kh), waits=[m_ch], mile=True)
                done["kt_free"] = m_sb
                done[("ksq", hc)] = m_sb
            PR.wait("tensor", done[("fvdone", hsl)])
            for cc in range(NCH):
                bank = cc % 2
                m_mm = None
                for pi in range(3):
                    wtile_ = (fv_hi if pi < 2 else fv_lo)(slot)
                    kr_ = ksqh if pi != 1 else ksql
                    for hc in range(4):
                        first = (pi == 0 and hc == 0)
                        last = (pi == 2 and hc == 3)
                        m_mm = te(lambda e, hc=hc, cc=cc, bank=bank,
                                  wtile_=wtile_, kr_=kr_, first=first,
                                  last=last: e.matmul(
                                      pex[bank],
                                      wtile_[:, hc * C + cc * P:
                                             hc * C + (cc + 1) * P],
                                      kr_[:, hc * SEG:(hc + 1) * SEG],
                                      start=first, stop=last),
                                  waits=([done.get(("pex_free", bank))] +
                                         [done[("ksq", h2)]
                                          for h2 in range(4)]
                                         if first else ()),
                                  mile=last)
                cv = cmac[:, cc * TLOC + g * SEG:cc * TLOC + (g + 1) * SEG]
                if hsl == 0:
                    m_cm = ve(lambda e, cv=cv, bank=bank: e.tensor_copy(
                        cv, pex[bank]), waits=[m_mm], mile=True)
                else:
                    m_cm = ve(lambda e, cv=cv, bank=bank: e.tensor_add(
                        cv, cv, pex[bank]), waits=[m_mm], mile=True)
                done[("pex_free", bank)] = m_cm
                done[("cm", g, cc)] = m_cm
            for hc in range(4):
                done[("ksq_free", hc)] = done[("cm", g, NCH - 1)]
            if em_next is not None:
                em_next(4)
        done[("slab_free", slot)] = done[("cm", NSEG - 1, NCH - 1)]
    if "cm" in dbg:
        gpd("s_dbg", lambda e: e.dma_start(out=dbg["cm"], in_=cmac),
            waits=[done[("cm", NSEG - 1, NCH - 1)]])
    m_slabs_done = done[("cm", NSEG - 1, NCH - 1)]
    AR.pop()

    AR.push()
    frp_h = AR.alloc16(NCH * C)
    frp_l = AR.alloc16(NCH * C)
    wrb = AR.alloc(NCH * SEG)
    st3 = alloc_stats()
    sigf = [AR.alloc(SEG) for _ in range(2)]
    xtr = [AR.alloc(SEG) for _ in range(2)]
    x1st = [AR.alloc(SEG) for _ in range(2)]
    stgf = AR.alloc(SEG)

    frTr = frT.rearrange("(dc p) c -> p dc c", p=P)
    frstg = (stgf, stg3)
    m_prevf = [None, None]
    for ci in range(16):
        sl = ci % 2
        dc, half = divmod(ci, 2)
        off = dc * C + half * SEG
        src = frTr[:, dc, half * SEG:(half + 1) * SEG]
        wts = [m_prevf[sl]] if m_prevf[sl] else [m_slabs_done]
        m_l = ld("s_fr", lambda e, src=src, sl=sl: e.dma_start(
            out=frstg[sl], in_=src), waits=wts)
        m_c = sc(lambda e, off=off, sl=sl: e.copy(
            frp_h[:, off:off + SEG], frstg[sl]), waits=[m_l], mile=True)
        m_prevf[sl] = ve(lambda e, off=off, sl=sl: e.tensor_sub(
            frp_l[:, off:off + SEG], frstg[sl], frp_h[:, off:off + SEG]),
            waits=[m_c], mile=True)
    m_frdone = m_prevf[15 % 2]

    def frtile(wb, dc, cc):
        return wb[:, dc * C + cc * P:dc * C + (cc + 1) * P]

    m_cap = None
    for g in range(NSEG):
        m_wl = ld("s_wrb", lambda e, g=g: e.dma_start(
            out=wrb.rearrange("p (cc t) -> p cc t", cc=NCH),
            in_=sp_w.rearrange("p (cc t) -> p cc t", cc=NCH)[
                :, :, g * SEG:(g + 1) * SEG]),
            waits=[done.get("wrb_free"), m_slabs_done] + p2_w_done)
        PR.wait("tensor", m_frdone, done[("xmc", g)])
        for cc in range(NCH):
            bank = cc % 2
            par = cc % 2
            m_mm = None
            for pi in range(3):
                wb_ = frp_h if pi < 2 else frp_l
                xr_ = xmchr if pi != 1 else xmclr
                for dc in range(NCH):
                    first = (pi == 0 and dc == 0)
                    last = (pi == 2 and dc == NCH - 1)
                    m_mm = te(lambda e, dc=dc, cc=cc, g=g, bank=bank,
                              wb_=wb_, xr_=xr_, first=first, last=last:
                              e.matmul(
                                  pmm[bank], frtile(wb_, dc, cc),
                                  xr_[:, dc, g * SEG:(g + 1) * SEG],
                                  start=first, stop=last),
                        waits=([done.get(("pmm_free", bank))]
                               if first else ()),
                        mile=last)
            done["fr_last"] = m_mm
            m_e = sc(lambda e, bank=bank, par=par: e.activation(
                sigf[par], pmm[bank], AF.Sigmoid),
                waits=[m_mm, done.get(("sigf_free", par))], mile=True)
            done[("pmm_free", bank)] = m_e
            wv_ = wrb[:, cc * SEG:(cc + 1) * SEG]
            cv = cmac[:, cc * TLOC + g * SEG:cc * TLOC + (g + 1) * SEG]
            PR.wait("vector", m_wl, done[("cm", g, cc)], m_e)
            ve(lambda e, cv=cv, par=par: e.tensor_mul(st3["sq1"][:, :SEG],
                                                      sigf[par], cv))
            m_ya = ve(lambda e, wv_=wv_: e.tensor_add(
                wv_, wv_, st3["sq1"][:, :SEG]), mile=True)
            done[("sigf_free", par)] = m_ya
            done[("ya", g, cc)] = m_ya

        def yv(dc):
            return wrb[:, dc * SEG:(dc + 1) * SEG]
        m_st = emit_ln_stats(yv, SEG, st3,
                             [done[("ya", g, cc)] for cc in range(NCH)])
        for cc in range(NCH):
            par = cc % 2
            m_xl = ld(f"s_xt{par}", lambda e, g=g, cc=cc, par=par:
                      e.dma_start(
                          out=xtr[par],
                          in_=sp_xT.rearrange("p (dc t) -> p dc t",
                                              dc=NCH)[:, cc,
                                                      g * SEG:(g + 1) * SEG]),
                      waits=[done.get(("xtr_free", par)), p1_xT_done,
                             m_slabs_done])
            ve(lambda e, cc=cc: e.tensor_mul(st3["sq0"][:, :SEG], yv(cc),
                                             st3["sB"][:, :SEG]),
               waits=[m_st] if cc == 0 else ())
            ve(lambda e: e.tensor_sub(st3["sq0"][:, :SEG],
                                      st3["sq0"][:, :SEG],
                                      st3["msB"][:, :SEG]))
            PR.wait("vector", m_xl, done.get(("x1st_free", par)))
            m_x1 = ve(lambda e, par=par: e.tensor_add(
                x1st[par], st3["sq0"][:, :SEG], xtr[par]), mile=True)
            done[("xtr_free", par)] = m_x1
            m_xsp = gpd(f"s_spx1{par}", lambda e, g=g, cc=cc, par=par:
                        e.dma_start(
                            out=sp_x1.rearrange("p (cc t) -> p cc t",
                                                cc=NCH)[:, cc,
                                                        g * SEG:(g + 1) * SEG],
                            in_=x1st[par]), waits=[m_x1])
            if "x1" in dbg:
                m_xsp = gpd(f"s_spx1{par}", lambda e, g=g, cc=cc, par=par:
                            e.dma_start(
                                out=dbg["x1"].rearrange(
                                    "p (cc t) -> p cc t", cc=NCH)[
                                    :, cc, g * SEG:(g + 1) * SEG],
                                in_=x1st[par]), waits=[m_x1])
            done[("x1st_free", par)] = m_xsp
            if g == NSEG - 1:
                m_cap = ve(lambda e, cc=cc, par=par: e.tensor_copy(
                    x1L[:, cc:cc + 1], x1st[par][:, SEG - 1:SEG]),
                    waits=[m_x1], mile=(cc == NCH - 1))
        done["wrb_free"] = done[("x1st_free", (NCH - 1) % 2)]
        done["st3_last"] = m_x1
    m_ag3s = gpd("s_ag", lambda e: e.dma_start(out=ag3_in, in_=x1L),
                 waits=[m_cap])
    m_cc3 = gpc(lambda e: e.collective_compute(
        "AllGather", AL.bypass, replica_groups=[list(range(NCORES))],
        ins=[ag3_in.opt()], outs=[ag3_out.opt()]), waits=[m_ag3s])
    AR.pop()
    AR.pop()
    p3_x1_done = [sem("s_spx10").total(), sem("s_spx11").total()]

    AR.push()
    w3rp = AR.alloc16(2 * NCH * C)
    w3kp = AR.alloc16(2 * NCH * C)
    w3vp = AR.alloc16(2 * NCH * C)
    x1b = AR.alloc(NCH * SW)
    z1b = AR.alloc(NCH * SW)
    AR.off = (AR.off + 7) // 8 * 8
    xrb_off = AR.off
    xrb = AR.alloc(NCH * SEG)
    xkb_off = AR.off
    xkb = AR.alloc(NCH * SEG)
    xrb16 = AR.ap16[:, 2 * xrb_off:2 * xrb_off + 2 * NCH * SEG]
    xkb16 = AR.ap16[:, 2 * xkb_off:2 * xkb_off + 2 * NCH * SEG]
    xr_hi = xrb16[:, 0:NCH * SEG]
    xr_lo = xrb16[:, NCH * SEG:2 * NCH * SEG]
    xk_hi = xkb16[:, 0:NCH * SEG]
    xk_lo = xkb16[:, NCH * SEG:2 * NCH * SEG]
    st4 = alloc_stats()
    outS = [AR.alloc(C) for _ in range(2)]

    def sigslot(cc):
        return z1b[:, cc * SW:cc * SW + SEG]
    zsl = AR.alloc(NCH)
    mxs = AR.alloc(NCH * SEG)
    ktile = AR.alloc(SEG)
    stgw = [AR.alloc(SEG) for _ in range(2)]

    def hi_(wp):
        return wp[:, 0:NCH * C]

    def lo_(wp):
        return wp[:, NCH * C:2 * NCH * C]

    m_w3d = []
    m_prevw = [None, None]
    for wi, (wp, wsrc) in enumerate(((w3rp, w3rT), (w3kp, w3kT),
                                     (w3vp, w3vT))):
        wTr = wsrc.rearrange("(dc p) c -> p dc c", p=P)
        for ci in range(16):
            gi = wi * 16 + ci
            sl = gi % 2
            dc, half = divmod(ci, 2)
            off = dc * C + half * SEG
            src = wTr[:, dc, half * SEG:(half + 1) * SEG]
            wts = [m_prevw[sl]] if m_prevw[sl] else []
            if gi < 2:
                wts += [done.get("wrb_free")] + p3_x1_done
            m_l = ld("s_w3w", lambda e, src=src, sl=sl: e.dma_start(
                out=stgw[sl], in_=src), waits=wts)
            m_c = sc(lambda e, wp=wp, off=off, sl=sl: e.copy(
                hi_(wp)[:, off:off + SEG], stgw[sl]), waits=[m_l],
                mile=True)
            m_prevw[sl] = ve(lambda e, wp=wp, off=off, sl=sl: e.tensor_sub(
                lo_(wp)[:, off:off + SEG], stgw[sl],
                hi_(wp)[:, off:off + SEG]), waits=[m_c], mile=True)
        m_w3d.append(m_prevw[(wi * 16 + 15) % 2])

    if "w3d" in dbg:
        m_pd = None
        for wi, wp in enumerate((w3rp, w3kp, w3vp)):
            for ci in range(16):
                gi = wi * 16 + ci
                off = ci * SEG
                m_a = ve(lambda e, wp=wp, off=off: e.tensor_add(
                    ktile, hi_(wp)[:, off:off + SEG],
                    lo_(wp)[:, off:off + SEG]),
                    waits=[m_w3d[2], m_pd], mile=True)
                m_pd = gpd("s_dbg", lambda e, gi=gi: e.dma_start(
                    out=dbg["w3d"][:, gi * SEG:(gi + 1) * SEG],
                    in_=ktile), waits=[m_a])
        PR.wait("scalar", m_pd)

    m_x1l = emit_sel(ag3_out, selprev, scC[:, 0:NCH], m_cc3,
                     extra=[m_un])

    for g in range(NSEG):
        x1r = x1b.rearrange("p (dc w) -> p dc w", w=SW)
        spx1 = sp_x1.rearrange("p (cc t) -> p cc t", cc=NCH)
        wts = [done.get("x1b_free")] + p3_x1_done
        if g == 0:
            m_l = ld("s_x1b", lambda e, x1r=x1r, spx1=spx1: e.dma_start(
                out=x1r[:, :, 1:SW], in_=spx1[:, :, 0:SEG]), waits=wts)
            m_l2 = ve(lambda e, x1r=x1r: e.tensor_copy(
                x1r[:, :, 0:1],
                scC[:, 0:NCH].rearrange("p (dc w) -> p dc w", w=1)),
                waits=[m_l, m_x1l], mile=True)
            m_l2 = ve(lambda e: e.engine_nop(), waits=[m_l], mile=True)
        else:
            m_l = ld("s_x1b", lambda e, x1r=x1r, spx1=spx1, g=g: e.dma_start(
                out=x1r, in_=spx1[:, :, g * SEG - 1:(g + 1) * SEG]),
                waits=wts)
            m_l2 = ve(lambda e: e.engine_nop(), waits=[m_l], mile=True)

        def x1v(dc):
            return x1b[:, dc * SW + 1:(dc + 1) * SW]
        m_st = emit_ln_stats(x1v, SEG, st4, [m_l2])
        if g == 0:
            m_z0 = emit_onecol_ln(scC[:, 0:NCH], scD[:, 0:NCH], None,
                                  m_x1l, extra=[m_wrs])
            m_zm = ve(lambda e: e.tensor_copy(
                z1b.rearrange("p (dc w) -> p dc w", w=SW)[:, :, 0:1],
                scD[:, 0:NCH].rearrange("p (dc w) -> p dc w", w=1)),
                waits=[m_z0, done.get("z1b_free"),
                       done.get("z1b_free2")], mile=True)
        else:
            m_zm = ve(lambda e: e.tensor_copy(
                z1b.rearrange("p (dc w) -> p dc w", w=SW)[:, :, 0:1],
                zsl.rearrange("p (dc w) -> p dc w", w=1)),
                waits=[done[("zsl", g - 1)], done.get("z1b_free"),
                       done.get("z1b_free2")], mile=True)
        m_z1 = None
        for dc in range(NCH):
            ve(lambda e, dc=dc: e.tensor_mul(st4["sq0"][:, :SEG], x1v(dc),
                                             st4["sB"][:, :SEG]),
               waits=([m_st, done.get("z1b_free"),
                       done.get("z1b_free2")] if dc == 0 else ()))
            m_z1 = ve(lambda e, dc=dc: e.tensor_sub(
                z1b[:, dc * SW + 1:(dc + 1) * SW], st4["sq0"][:, :SEG],
                st4["msB"][:, :SEG]), mile=(dc == NCH - 1))
        done[("zsl", g)] = ve(lambda e: e.tensor_copy(
            zsl.rearrange("p (dc w) -> p dc w", w=1),
            z1b.rearrange("p (dc w) -> p dc w", w=SW)[:, :, SW - 1:SW]),
            waits=[m_z1], mile=True)
        m_xr = None
        m_xk = None
        for dc in range(NCH):
            zc = z1b[:, dc * SW + 1:(dc + 1) * SW]
            zp = z1b[:, dc * SW:(dc + 1) * SW - 1]
            xrv = mxs[:, dc * SEG:(dc + 1) * SEG]
            ve(lambda e, xrv=xrv, zp=zp, dc=dc: e.tensor_scalar(
                xrv, zp, OMUR[:, dc:dc + 1], None, AL.mult),
               waits=([m_z1, m_zm, done.get("mxs_free")] if dc == 0 else ()))
            m_xr = ve(lambda e, xrv=xrv, zc=zc, dc=dc:
                      e.scalar_tensor_tensor(
                          xrv, zc, mur[:, dc:dc + 1], xrv, AL.mult, AL.add),
                      mile=(dc == NCH - 1))
        if "xr" in dbg:
            m_xr = gpd("s_dbg", lambda e, g=g: e.dma_start(
                out=dbg["xr"][:, g * NCH * SEG:(g + 1) * NCH * SEG],
                in_=mxs), waits=[m_xr])
        m_cxr = None
        for dc in range(NCH):
            m_cxr = sc(lambda e, dc=dc: e.copy(
                xr_hi[:, dc * SEG:(dc + 1) * SEG],
                mxs[:, dc * SEG:(dc + 1) * SEG]),
                waits=([m_xr, done.get("xrb_free")] if dc == 0 else ()),
                mile=(dc == NCH - 1))
        m_sxr = None
        for dc in range(NCH):
            m_sxr = ve(lambda e, dc=dc: e.tensor_sub(
                xr_lo[:, dc * SEG:(dc + 1) * SEG],
                mxs[:, dc * SEG:(dc + 1) * SEG],
                xr_hi[:, dc * SEG:(dc + 1) * SEG]),
                waits=([m_cxr] if dc == 0 else ()), mile=(dc == NCH - 1))
        for dc in range(NCH):
            zc = z1b[:, dc * SW + 1:(dc + 1) * SW]
            zp = z1b[:, dc * SW:(dc + 1) * SW - 1]
            xkv = mxs[:, dc * SEG:(dc + 1) * SEG]
            ve(lambda e, xkv=xkv, zp=zp, dc=dc: e.tensor_scalar(
                xkv, zp, OMUK[:, dc:dc + 1], None, AL.mult),
               waits=([m_cxr] if dc == 0 else ()))
            m_xk = ve(lambda e, xkv=xkv, zc=zc, dc=dc:
                      e.scalar_tensor_tensor(
                          xkv, zc, muk[:, dc:dc + 1], xkv, AL.mult, AL.add),
                      mile=(dc == NCH - 1))
        if "xr" in dbg:
            m_xk = gpd("s_dbg", lambda e, g=g: e.dma_start(
                out=dbg["xk"][:, g * NCH * SEG:(g + 1) * NCH * SEG],
                in_=mxs), waits=[m_xk])
        m_cxk = None
        for dc in range(NCH):
            m_cxk = sc(lambda e, dc=dc: e.copy(
                xk_hi[:, dc * SEG:(dc + 1) * SEG],
                mxs[:, dc * SEG:(dc + 1) * SEG]),
                waits=([m_xk, done.get("xkb_free")] if dc == 0 else ()),
                mile=(dc == NCH - 1))
        m_sxk = None
        for dc in range(NCH):
            m_sxk = ve(lambda e, dc=dc: e.tensor_sub(
                xk_lo[:, dc * SEG:(dc + 1) * SEG],
                mxs[:, dc * SEG:(dc + 1) * SEG],
                xk_hi[:, dc * SEG:(dc + 1) * SEG]),
                waits=([m_cxk] if dc == 0 else ()), mile=(dc == NCH - 1))
        done["mxs_free"] = m_cxk

        def p4_group(wp, xh, xl, cc, bank, psb, first_extra):
            m_mm = None
            for pi in range(3):
                wb_ = hi_(wp) if pi < 2 else lo_(wp)
                xx = xh if pi != 1 else xl
                for dc in range(NCH):
                    first = (pi == 0 and dc == 0)
                    last = (pi == 2 and dc == NCH - 1)
                    m_mm = te(lambda e, dc=dc, cc=cc, bank=bank, wb_=wb_,
                              xx=xx, first=first, last=last, psb=psb:
                              e.matmul(
                                  psb[bank], wtile(wb_, dc, cc),
                                  xx[:, dc * SEG:(dc + 1) * SEG],
                                  start=first, stop=last),
                              waits=(([done.get(("pmm_free", bank))] +
                                      first_extra) if first else ()),
                              mile=last)
            return m_mm

        PR.wait("tensor", m_w3d[0], m_sxr)
        for cc in range(NCH):
            bank = cc % 2
            m_mm = p4_group(w3rp, xr_hi, xr_lo, cc, bank, pmm, [])
            m_e = sc(lambda e, bank=bank, cc=cc: e.activation(
                sigslot(cc), pmm[bank], AF.Sigmoid),
                waits=[m_mm, m_xk], mile=True)
            done[("pmm_free", bank)] = m_e
            done[("sig", g, cc)] = m_e
            if "xr" in dbg:
                done["z1b_free2"] = gpd(
                    "s_dbg", lambda e, g=g, cc=cc: e.dma_start(
                        out=dbg["sg"][:, (g * NCH + cc) * SEG:
                                      (g * NCH + cc + 1) * SEG],
                        in_=sigslot(cc)), waits=[m_e])
        PR.wait("tensor", m_w3d[1], m_sxk)
        for cc in range(NCH):
            bank = cc % 2
            m_mm = p4_group(w3kp, xk_hi, xk_lo, cc, bank, pmm, [])
            m_rl = sc(lambda e, bank=bank: e.activation(
                ktile, pmm[bank], AF.Relu),
                waits=[m_mm, done.get("ktile_free")], mile=True)
            done[("pmm_free", bank)] = m_rl
            m_sq = ve(lambda e: e.tensor_mul(ktile, ktile, ktile),
                      waits=[m_rl], mile=True)
            if "xr" in dbg:
                m_sq = gpd("s_dbg", lambda e, g=g, cc=cc: e.dma_start(
                    out=dbg["ka"][:, (g * NCH + cc) * SEG:
                                  (g * NCH + cc + 1) * SEG],
                    in_=ktile), waits=[m_sq])
            kh = xr_hi[:, cc * SEG:(cc + 1) * SEG]
            kl = xr_lo[:, cc * SEG:(cc + 1) * SEG]
            m_ch = sc(lambda e, kh=kh: e.copy(kh, ktile),
                      waits=[m_sq, done.get(("ka_free", cc))], mile=True)
            m_ka = ve(lambda e, kl=kl, kh=kh: e.tensor_sub(kl, ktile, kh),
                      waits=[m_ch], mile=True)
            done["ktile_free"] = m_ka
            done[("ka", g, cc)] = m_ka
        PR.wait("tensor", m_w3d[2],
                *[done[("ka", g, cc)] for cc in range(NCH)])
        for cc in range(NCH):
            bank = cc % 2
            m_mm = p4_group(w3vp, xr_hi, xr_lo, cc, bank, pmm, [])
            yfv = xkb[:, cc * SEG:(cc + 1) * SEG]
            m_yf = ve(lambda e, yfv=yfv, bank=bank, cc=cc: e.tensor_mul(
                yfv, sigslot(cc), pmm[bank]),
                waits=[m_mm, done[("sig", g, cc)]], mile=True)
            done[("pmm_free", bank)] = m_yf
            done[("yf", g, cc)] = m_yf

        if "xr" in dbg:
            m_yfd = gpd("s_dbg", lambda e, g=g: e.dma_start(
                out=dbg["yf"][:, g * NCH * SEG:(g + 1) * NCH * SEG],
                in_=xkb), waits=[done[("yf", g, NCH - 1)]])
            PR.wait("vector", m_yfd)

        def yfv_(dc):
            return xkb[:, dc * SEG:(dc + 1) * SEG]
        m_st = emit_ln_stats(yfv_, SEG, st4,
                             [done[("yf", g, cc)] for cc in range(NCH)])
        m_out = None
        for cc in range(NCH):
            ve(lambda e, cc=cc: e.tensor_mul(st4["sq0"][:, :SEG], yfv_(cc),
                                             st4["sB"][:, :SEG]),
               waits=[m_st] if cc == 0 else ())
            ve(lambda e: e.tensor_sub(st4["sq0"][:, :SEG],
                                      st4["sq0"][:, :SEG],
                                      st4["msB"][:, :SEG]))
            x1v_ = x1b[:, cc * SW + 1:(cc + 1) * SW]
            m_out = ve(lambda e, cc=cc, x1v_=x1v_: e.tensor_add(
                xkb[:, cc * SEG:(cc + 1) * SEG], st4["sq0"][:, :SEG],
                x1v_), mile=(cc == NCH - 1))
        done["x1b_free"] = m_out
        done["xrb_free"] = m_out
        done["z1b_free"] = done[("yf", g, NCH - 1)]
        for cc in range(NCH):
            done[("ka_free", cc)] = m_out
        for tb in range(NTB):
            slot = tb % 2
            bankA = trp[0] % 2
            trp[0] += 1
            m_tA = None
            wts = [done.get(("ptr_free", bankA)), m_out]
            for dc in range(4):
                m_tA = te(lambda e, dc=dc, tb=tb, bankA=bankA: e.transpose(
                    ptr[bankA][:, dc * P:(dc + 1) * P],
                    xkb[:, dc * SEG + tb * P:dc * SEG + (tb + 1) * P],
                    ident_t), waits=wts, mile=(dc == 3))
                wts = []
            m_evA = sc(lambda e, bankA=bankA, slot=slot: e.copy(
                outS[slot][:, 0:4 * P], ptr[bankA][:, :]),
                waits=[m_tA, done.get(("outS_free", slot))], mile=True)
            done[("ptr_free", bankA)] = m_evA
            bankB = trp[0] % 2
            trp[0] += 1
            m_tB = None
            wts = [done.get(("ptr_free", bankB))]
            for dc in range(4, NCH):
                m_tB = te(lambda e, dc=dc, tb=tb, bankB=bankB: e.transpose(
                    ptr[bankB][:, (dc - 4) * P:(dc - 3) * P],
                    xkb[:, dc * SEG + tb * P:dc * SEG + (tb + 1) * P],
                    ident_t), waits=wts, mile=(dc == NCH - 1))
                wts = []
            m_evB = sc(lambda e, bankB=bankB, slot=slot: e.copy(
                outS[slot][:, 4 * P:C], ptr[bankB][:, :]),
                waits=[m_tB], mile=True)
            done[("ptr_free", bankB)] = m_evB
            m_os = gpd(f"s_out{slot}", lambda e, g=g, tb=tb, slot=slot:
                       e.dma_start(
                           out=out_d[g * SEG + tb * P:
                                     g * SEG + (tb + 1) * P, :],
                           in_=outS[slot]), waits=[m_evA, m_evB])
            done[("outS_free", slot)] = m_os
        done["xkb_free"] = m_tB
    AR.pop()

    finals = [sems[k].total() for k in sems
              if k.startswith(("s_out", "s_sp", "s_dbg", "s_ag"))]
    finals.append(s_cc.total())
    PR.wait("gpsimd", *finals)

    with nc.Block() as block:
        PR.install(block)
    return nc


def _prep_inputs(inputs):
    x = np.asarray(inputs["L0_attn_x"], np.float32)[0]
    zrow = np.zeros((1, C), np.float32)
    xpad = np.concatenate([zrow, x, zrow], axis=0)

    def tr(a):
        return np.ascontiguousarray(np.asarray(a, np.float32).T)

    wkT = tr(inputs["L0_attn_Wk"]); wvT = tr(inputs["L0_attn_Wv"])
    wrT = tr(inputs["L0_attn_Wr"]); woT = tr(inputs["L0_attn_Wo"])
    fkT = tr(inputs["L0_attn_Fk"]); fvT = tr(inputs["L0_attn_Fv"])
    frT = tr(inputs["L0_attn_Fr"])
    w3rT = tr(inputs["L0_ffn_W_r"]); w3kT = tr(inputs["L0_ffn_W_k"])
    w3vT = tr(inputs["L0_ffn_W_v"])

    def col(v):
        return np.ascontiguousarray(np.asarray(v, np.float32).reshape(
            NCH, P).T)

    pv = np.concatenate([
        col(inputs["L0_attn_ln1_g"]), col(inputs["L0_attn_ln1_b"]),
        col(inputs["L0_attn_ln2_g"]), col(inputs["L0_attn_ln2_b"]),
        col(inputs["L0_attn_att_time_mix"]),
        col(inputs["L0_attn_att_combined_mix"]),
        col(inputs["L0_attn_time_decay"]),
        col(inputs["L0_attn_time_first"]),
        col(inputs["L0_attn_ffn_time_mix"]),
        col(inputs["L0_attn_ffn_combined_mix"]),
        col(inputs["L0_ffn_mu_r"]), col(inputs["L0_ffn_mu_k"]),
    ], axis=1).astype(np.float32)

    ones_d = np.ones((P, P), np.float32)
    ident_d = np.eye(P, dtype=np.float32)

    in_maps = []
    for i in range(NCORES):
        sel_prev = np.zeros((P, NCORES), np.float32)
        sel_next = np.zeros((P, NCORES), np.float32)
        if i > 0:
            sel_prev[:, i - 1] = 1.0
        if i < NCORES - 1:
            sel_next[:, i + 1] = 1.0
        mk = np.zeros((P, 2), np.float32)
        mk[:, 0] = 1.0 if i > 0 else 0.0
        mk[:, 1] = 1.0 if i < NCORES - 1 else 0.0
        msk = np.concatenate([sel_prev, sel_next, mk], axis=1)
        xsl = np.ascontiguousarray(xpad[i * TLOC:(i + 1) * TLOC + 2])
        in_maps.append({
            "xs": xsl, "wkT": wkT, "wvT": wvT, "wrT": wrT, "woT": woT,
            "fkT": fkT, "fvT": fvT, "frT": frT,
            "w3rT": w3rT, "w3kT": w3kT, "w3vT": w3vT,
            "pv": pv, "masks": msk, "ones_d": ones_d, "ident_d": ident_d,
        })
    return in_maps


def kernel(**inputs):
    in_maps = _prep_inputs(inputs)
    nc = build_nc()
    res = run_bass_kernel_spmd(nc, in_maps, core_ids=list(range(NCORES)))
    out = np.concatenate([res.results[i]["out"] for i in range(NCORES)],
                         axis=0)
    return out[None, :, :]
